# version 01
# baseline (speedup 1.0000x reference)
import numpy as np
import ml_dtypes
import concourse.bass as bass
import concourse.mybir as mybir
from concourse.bass_utils import run_bass_kernel_spmd

F32 = mybir.dt.float32
BF16 = mybir.dt.bfloat16
AF = mybir.ActivationFunctionType
ALU = mybir.AluOpType
AX = mybir.AxisListType

T = 2048
NT = 16
D = 2048
NI = 16
NG = 4
HPG = 4
HID = 8192
SW = 132
SWALL = HPG * SW

DEBUG = False


class Prog:
    ENGINES = ("sync", "gpsimd", "vector", "scalar", "tensor")
    AUTO = {"vector": "dv", "scalar": "ac", "gpsimd": "gp"}

    def __init__(self):
        self.items = {e: [] for e in self.ENGINES}
        self.counts = {n: 0 for n in ("qs", "qg", "pe", "dv", "ac", "gp")}

    def add(self, eng, fn, waits=(), inc=None, amt=1, is_dma=False):
        w = {}
        for t in waits:
            if t is None:
                continue
            s, v = t
            if v > 0:
                w[s] = max(w.get(s, 0), v)
        if eng in self.AUTO and not is_dma:
            s = self.AUTO[eng]
            if self.counts[s] > 0:
                w[s] = max(w.get(s, 0), self.counts[s])
            inc, amt = s, 1
        tok = None
        rec = None
        if inc is not None:
            self.counts[inc] += amt
            tok = (inc, self.counts[inc])
            rec = (inc, amt)
        self.items[eng].append((fn, sorted(w.items()), rec))
        return tok

    def dma(self, eng, fn, waits=()):
        q = "qs" if eng in ("sync", "scalar") else "qg"
        return self.add(eng, fn, waits, q, 16, is_dma=True)

    def pe_drain(self, waits=()):
        return self.add("tensor", lambda e: e.drain(fusable=False), waits, inc="pe")


def build(debug=DEBUG):
    nc = bass.Bass("TRN2", target_bir_lowering=False, debug=False)
    p = Prog()

    x_d = nc.dram_tensor("x", [T, D], F32, kind="ExternalInput").ap()
    xt_d = nc.dram_tensor("xt", [D, T], F32, kind="ExternalInput").ap()
    wq_d = nc.dram_tensor("wq", [16, 128, 2048], F32, kind="ExternalInput").ap()
    wk_d = nc.dram_tensor("wk", [4, 128, 8192], F32, kind="ExternalInput").ap()
    wv_d = nc.dram_tensor("wv", [4, 128, 8192], F32, kind="ExternalInput").ap()
    wo_d = nc.dram_tensor("wo", [16, 128, 2048], F32, kind="ExternalInput").ap()
    wg_d = nc.dram_tensor("wg", [64, 128, 2048], F32, kind="ExternalInput").ap()
    wu_d = nc.dram_tensor("wu", [64, 128, 2048], F32, kind="ExternalInput").ap()
    wd_d = nc.dram_tensor("wd", [16, 128, 8192], F32, kind="ExternalInput").ap()
    bqp_d = nc.dram_tensor("bqp", [128, 16], F32, kind="ExternalInput").ap()
    bop_d = nc.dram_tensor("bop", [128, 16], F32, kind="ExternalInput").ap()
    bk_d = nc.dram_tensor("bk", [1, 2048], F32, kind="ExternalInput").ap()
    bv_d = nc.dram_tensor("bv", [1, 2048], F32, kind="ExternalInput").ap()
    rm_d = nc.dram_tensor("rmask", [128, 8], F32, kind="ExternalInput").ap()

    y_d = nc.dram_tensor("y", [D, T], F32, kind="ExternalOutput").ap()
    attn_kind = "ExternalOutput" if debug else "Internal"
    attn_sc = nc.dram_tensor("attn_sc", [16, 128, T], BF16, kind=attn_kind).ap()
    if debug:
        dbg_xn = nc.dram_tensor("dbg_xn", [128, NI * T], BF16, kind="ExternalOutput").ap()
        dbg_kf = nc.dram_tensor("dbg_kf", [128, NT * 512], BF16, kind="ExternalOutput").ap()
        dbg_vc = nc.dram_tensor("dbg_vc", [128, NT * 512], BF16, kind="ExternalOutput").ap()
        dbg_qf = nc.dram_tensor("dbg_qf", [128, HPG * T], BF16, kind="ExternalOutput").ap()
        dbg_st = nc.dram_tensor("dbg_st", [4, 128, SWALL], F32, kind="ExternalOutput").ap()
        dbg_y15 = nc.dram_tensor("dbg_y15", [D, 512], F32, kind="ExternalOutput").ap()
        dbg_gated = nc.dram_tensor("dbg_gated", [128, 32768], BF16, kind="ExternalOutput").ap()

    cc_in = [nc.dram_tensor(f"cc_in{g}", [128, SWALL], F32, kind="Internal").ap()
             for g in range(NG)]
    cc_out = [nc.dram_tensor(f"cc_out{g}", [8 * 128, SWALL], F32, kind="Internal").ap()
              for g in range(NG)]

    from contextlib import ExitStack
    with ExitStack() as _stack:
        def sb(name, shape, dtype):
            return _stack.enter_context(nc.sbuf_tensor(name, shape, dtype)).ap()

        xn_fm = sb("xn_fm", [128, NI * T], BF16)
        abf = sb("abf", [128, 49152], BF16)
        af32 = sb("af32", [128, 6144], F32)
        ones_row = sb("ones_row", [1, 128], BF16)
        ones_col = sb("ones_col", [128, 1], BF16)
        mask4 = sb("mask4", [128, 512], BF16)
        ident = sb("ident", [128, 128], BF16)
        eps_ln = sb("eps_ln", [128, 1], F32)
        neghalf = sb("neghalf", [128, 1], F32)
        stats = sb("stats", [128, 16], F32)
        sc1 = sb("sc1", [128, 512], F32)
        sc2 = sb("sc2", [128, 512], F32)
        sc3 = sb("sc3", [128, 512], F32)
        state = sb("state", [128, SWALL], F32)
        state_bf = sb("state_bf", [128, SWALL], BF16)
        cc_sb = sb("cc_sb", [128, SWALL], F32)
        gath = sb("gath", [128, SWALL], F32)
        astage = sb("astage", [128, 2 * 512], BF16)
        A_m4 = sb("A_m4", [128, 512], BF16)
        kf_fm4 = sb("kf_fm4", [128, 512], BF16)
        attn_tm4 = sb("attn_tm4", [128, 512], BF16)
        ztmp = sb("ztmp", [128, 4], F32)
        rz = sb("rz", [128, 4], F32)
        bq_pp = sb("bq_pp", [128, 16], F32)
        bo_pp = sb("bo_pp", [128, 16], F32)
        bk_row = sb("bk_row", [1, 512], BF16)
        bv_row = sb("bv_row", [1, 512], BF16)
        rm_sb = sb("rm_sb", [128, 8], F32)

        s0 = _stack.enter_context(nc.psum_tensor("ps_s0", [128, 512], F32)).ap()
        s1 = _stack.enter_context(nc.psum_tensor("ps_s1", [128, 512], F32)).ap()
        s2 = _stack.enter_context(nc.psum_tensor("ps_s2", [128, 512], F32)).ap()
        s3 = _stack.enter_context(nc.psum_tensor("ps_s3", [128, 512], F32)).ap()
        d0 = _stack.enter_context(nc.psum_tensor("ps_d0", [128, 1024], F32)).ap()
        d1 = _stack.enter_context(nc.psum_tensor("ps_d1", [128, 1024], F32)).ap()
        d0b = d0.bitcast(BF16)
        d1b = d1.bitcast(BF16)
        s0b = s0.bitcast(BF16)
        s1b = s1.bitcast(BF16)

        sems = {}
        for name in ("qs", "qg", "pe", "dv", "ac", "gp"):
            sems[name] = _stack.enter_context(nc.semaphore(name))
        block = _stack.enter_context(nc.Block())

        kf_tm = abf[:, 0:8192]
        vc_tm = abf[:, 8192:16384]
        qf_fm = abf[:, 16384:24576]
        wk_sl = abf[:, 24576:32768]
        wv_sl = abf[:, 32768:40960]
        wq_sl = abf[:, 40960:49152]
        xn_tm = [abf[:, 45056:47104], abf[:, 47104:49152]]
        attn_res = abf[:, 0:8192]
        wo_sl = [abf[:, 8192:10240], abf[:, 10240:12288]]
        gated = abf[:, 0:32768]
        wg_sl = [abf[:, 32768:34816], abf[:, 34816:36864]]
        wu_sl = [abf[:, 36864:38912], abf[:, 38912:40960]]
        wd_sl = [abf[:, 40960:45056], abf[:, 45056:49152]]
        x_tile = [af32[:, 0:2048], af32[:, 2048:4096]]
        x2 = af32[:, 4096:6144]
        xr15 = [af32[:, 0:512], af32[:, 512:1024]]
        ysb15 = [af32[:, 1024:1536], af32[:, 1536:2048]]
        gsil = af32[:, 0:1024]
        yin2 = [af32[:, 1024:2048], af32[:, 2048:3072]]
        ysb2 = [af32[:, 3072:4096], af32[:, 4096:5120]]

        def MM(out, lhsT, rhs, start, stop, waits=()):
            return p.add(
                "tensor",
                lambda e, o=out, l=lhsT, r=rhs, s=start, t=stop: e.matmul(
                    o, l, r, start=s, stop=t),
                waits)

        def TR(out, in_, waits=()):
            return p.add(
                "tensor",
                lambda e, o=out, i=in_: e.transpose(o, i, ident),
                waits)

        p.add("gpsimd", lambda e: e.memset(ones_row[:, :], 1.0))
        p.add("gpsimd", lambda e: e.memset(ones_col[:, :], 1.0))
        p.add("gpsimd", lambda e: e.memset(mask4[:, :], 0.0))
        for hh in range(4):
            p.add("gpsimd", lambda e, o=mask4[:, hh * 128:(hh + 1) * 128]: e.affine_select(
                out=o, in_=o, compare_op=ALU.is_gt, fill=1.0,
                base=0, pattern=[[-1, 128]], channel_multiplier=1))
        p.add("gpsimd", lambda e: e.memset(ident[:, :], 0.0))
        p.add("gpsimd", lambda e: e.affine_select(
            out=ident[:, :], in_=ident[:, :], compare_op=ALU.not_equal, fill=1.0,
            base=0, pattern=[[-1, 128]], channel_multiplier=1))
        p.add("gpsimd", lambda e: e.memset(eps_ln[:, :], 1e-5))
        p.add("gpsimd", lambda e: e.memset(neghalf[:, :], -0.5))
        tok_const = p.add("gpsimd", lambda e: e.memset(cc_sb[:, :], 0.0))

        p.dma("sync", lambda e: e.dma_start(out=rm_sb[:, :], in_=rm_d))
        p.dma("sync", lambda e: e.dma_start(out=bq_pp[:, :], in_=bqp_d))
        p.dma("sync", lambda e: e.dma_start(out=bo_pp[:, :], in_=bop_d))

        xn3 = xn_fm.rearrange("p (ic t) -> p ic t", t=T)
        s0b3 = s0b[:, 0:1024].rearrange("p (ic c) -> p ic c", c=128)
        s1b3 = s1b[:, 0:1024].rearrange("p (ic c) -> p ic c", c=128)
        tok_xnm = [None] * NT
        tok_xn_tb = [None] * NT
        tok_x = [None] * NT
        prev_half_copy = [None, None]

        for tb in range(NT):
            xs = x_tile[tb % 2]
            tok_x[tb] = p.dma(
                "sync",
                lambda e, o=xs, i=x_d[tb * 128:(tb + 1) * 128, :]: e.dma_start(out=o, in_=i),
                waits=[tok_xnm[tb - 2] if tb >= 2 else None])
            st = stats[:, (tb % 2) * 8:(tb % 2) * 8 + 8]
            p.add("vector", lambda e, x=xs, o=x2: e.tensor_mul(o, x, x),
                  waits=[tok_x[tb]])
            p.add("vector", lambda e, x=xs, o=st[:, 0:1]: e.reduce_sum(o, x, axis=AX.X))
            p.add("vector", lambda e, o=st[:, 0:1]: e.tensor_scalar_mul(o, o, 1.0 / D))
            p.add("vector", lambda e, o=st[:, 1:2], i=x2: e.reduce_sum(o, i, axis=AX.X))
            p.add("vector", lambda e, o=st[:, 1:2]: e.tensor_scalar_mul(o, o, 1.0 / D))
            p.add("vector", lambda e, o=st[:, 2:3], m=st[:, 0:1]: e.tensor_mul(o, m, m))
            tok_var = p.add("vector", lambda e, o=st[:, 1:2], a=st[:, 2:3]: e.tensor_sub(o, o, a))
            p.add("scalar", lambda e, o=st[:, 3:4], i=st[:, 1:2]: e.activation(
                o, i, AF.Ln, bias=eps_ln[:, 0:1]), waits=[tok_var])
            tok_rstd = p.add("scalar", lambda e, o=st[:, 4:5], i=st[:, 3:4]: e.activation(
                o, i, AF.Exp, scale=neghalf[:, 0:1]))
            xnt = xn_tm[tb % 2]
            tok_xnm[tb] = p.add(
                "vector",
                lambda e, o=xnt, x=xs, m=st[:, 0:1], r=st[:, 4:5]: e.tensor_scalar(
                    o, x, m, r, op0=ALU.subtract, op1=ALU.mult),
                waits=[tok_rstd])
            for half in range(2):
                pb2 = [s0b, s1b][half]
                for j in range(8):
                    ic = half * 8 + j
                    w = []
                    if j == 0:
                        w = [tok_xnm[tb], prev_half_copy[half], tok_const]
                    TR(pb2[:, j * 128:(j + 1) * 128],
                       xnt[:, ic * 128:(ic + 1) * 128], waits=w)
                dtok = p.pe_drain()
                prev_half_copy[half] = p.add(
                    "vector",
                    lambda e, o=xn3[:, half * 8:(half + 1) * 8, tb * 128:(tb + 1) * 128],
                    i=[s0b3, s1b3][half][:, :, :]: e.tensor_copy(o, i),
                    waits=[dtok])
            tok_xn_tb[tb] = prev_half_copy[1]

        if debug:
            p.dma("sync", lambda e: e.dma_start(out=dbg_xn, in_=xn_fm[:, :]),
                  waits=[tok_xn_tb[NT - 1]])

        tok_wk = [None] * NG
        tok_wv = [None] * NG
        tok_wq = [None] * NG
        tok_cc = [None] * NG
        prev_p1a_drain = None
        prev_p1c_drain = None
        prev_group_pe = None
        prev_norm_last = None
        prev_cast_last = None
        prev_stage_last = None
        prev_ccin = None
        prev_ccsb = None
        tok_attn_dma_n = {}

        for g in range(NG):
            tok_wk[g] = p.dma("gpsimd",
                              lambda e, o=wk_sl, i=wk_d[g]: e.dma_start(out=o, in_=i),
                              waits=[prev_p1a_drain])
            tok_wv[g] = p.dma("gpsimd",
                              lambda e, o=wv_sl, i=wv_d[g]: e.dma_start(out=o, in_=i),
                              waits=[prev_p1a_drain])
            p.dma("gpsimd", lambda e, i=bk_d[:, g * 512:(g + 1) * 512]:
                  e.dma_start(out=bk_row[:, :], in_=i), waits=[prev_p1a_drain])
            tok_bkv = p.dma("gpsimd", lambda e, i=bv_d[:, g * 512:(g + 1) * 512]:
                            e.dma_start(out=bv_row[:, :], in_=i), waits=[prev_p1a_drain])
            for hh in range(HPG):
                jc = g * HPG + hh
                tok_wq[g] = p.dma(
                    "gpsimd",
                    lambda e, o=wq_sl[:, hh * 2048:(hh + 1) * 2048], i=wq_d[jc]:
                        e.dma_start(out=o, in_=i),
                    waits=[prev_p1c_drain,
                           tok_xn_tb[NT - 1] if g == 0 and hh == 0 else None])

            tok_pk = [None] * NT
            tok_pv = [None] * NT
            tok_ek = [None] * NT
            tok_ev = [None] * NT

            def emit_tot(tb):
                for hh in range(HPG):
                    bank = d0 if hh < 2 else d1
                    co = (hh % 2) * 512
                    w = []
                    if hh == 0:
                        w = [tok_ek[tb], tok_ev[tb]]
                        if tb == 0:
                            w += [prev_stage_last, prev_ccsb]
                    kt = kf_tm[:, tb * 512 + hh * 128: tb * 512 + (hh + 1) * 128]
                    vt = vc_tm[:, tb * 512 + hh * 128: tb * 512 + (hh + 1) * 128]
                    MM(bank[:, co:co + 128], kt, vt, start=(tb == 0), stop=False, waits=w)
                    MM(bank[:, co + 128:co + 129], kt, ones_col, start=False,
                       stop=(tb == NT - 1))

            for tb in range(NT):
                for i in range(NI):
                    w = []
                    if i == 0:
                        w = [tok_xn_tb[NT - 1] if tb == 0 else tok_xn_tb[tb],
                             tok_wk[g], tok_bkv, tok_ek[tb - 1] if tb >= 1 else None,
                             tok_const]
                        if tb == 0:
                            w.append(prev_norm_last)
                    MM(s0[:, 0:512], xn_fm[:, i * T + tb * 128: i * T + (tb + 1) * 128],
                       wk_sl[:, i * 512:(i + 1) * 512], start=(i == 0), stop=False, waits=w)
                MM(s0[:, 0:512], ones_row, bk_row[:, 0:512], start=False, stop=True)
                tok_pk[tb] = p.pe_drain()
                for i in range(NI):
                    w = [tok_wv[g], tok_ev[tb - 1] if tb >= 1 else None] if i == 0 else []
                    MM(s1[:, 0:512], xn_fm[:, i * T + tb * 128: i * T + (tb + 1) * 128],
                       wv_sl[:, i * 512:(i + 1) * 512], start=(i == 0), stop=False, waits=w)
                MM(s1[:, 0:512], ones_row, bv_row[:, 0:512], start=False, stop=True)
                tok_pv[tb] = p.pe_drain()
                if tb >= 1:
                    emit_tot(tb - 1)

                tk1 = p.add("vector", lambda e, o=sc1: e.tensor_scalar_min(o, s0[:, 0:512], 0.0),
                            waits=[tok_pk[tb]])
                tk2 = p.add("scalar", lambda e: e.activation(sc2, sc1, AF.Exp),
                            waits=[tk1])
                p.add("vector", lambda e, o=sc3: e.tensor_scalar_max(o, s0[:, 0:512], 0.0))
                tok_ek[tb] = p.add(
                    "vector",
                    lambda e, o=kf_tm[:, tb * 512:(tb + 1) * 512]: e.tensor_add(o, sc2, sc3),
                    waits=[tk2])
                tok_ev[tb] = p.add(
                    "vector",
                    lambda e, o=vc_tm[:, tb * 512:(tb + 1) * 512]: e.tensor_copy(o, s1[:, 0:512]),
                    waits=[tok_pv[tb]])
            emit_tot(NT - 1)
            tok_totdrain = p.pe_drain()
            prev_p1a_drain = tok_totdrain

            if debug and g == 0:
                p.dma("sync", lambda e: e.dma_start(out=dbg_kf, in_=kf_tm),
                      waits=[tok_ek[NT - 1]])
                p.dma("sync", lambda e: e.dma_start(out=dbg_vc, in_=vc_tm),
                      waits=[tok_ev[NT - 1]])

            tok_ccsb = None
            for hh in range(HPG):
                bank = d0 if hh < 2 else d1
                co = (hh % 2) * 512
                tok_ccsb = p.add(
                    "vector",
                    lambda e, o=cc_sb[:, hh * SW: hh * SW + 129], i=bank[:, co:co + 129]:
                        e.tensor_copy(o, i),
                    waits=[tok_totdrain, prev_ccin if hh == 0 else None])
            prev_ccsb = tok_ccsb
            t_ccin = p.dma("sync", lambda e, o=cc_in[g]: e.dma_start(out=o, in_=cc_sb[:, :]),
                           waits=[tok_ccsb])
            prev_ccin = t_ccin
            tok_cc[g] = p.add(
                "gpsimd",
                lambda e, i=cc_in[g], o=cc_out[g]: e.collective_compute(
                    "AllGather", ALU.bypass, replica_groups=[list(range(8))],
                    ins=[i.opt()], outs=[o.opt()]),
                waits=[t_ccin])

            tok_qevac = []
            tok_pq = []
            qi = 0
            for hh in range(HPG):
                jj = g * HPG + hh
                for tblk in range(4):
                    bank = [s2, s3][qi % 2]
                    for i in range(NI):
                        w = []
                        if i == 0:
                            w = [tok_wq[g],
                                 tok_qevac[qi - 2] if qi >= 2 else None]
                            if qi == 0:
                                w += [prev_norm_last]
                        MM(bank[:, 0:512],
                           wq_sl[:, hh * 2048 + i * 128: hh * 2048 + (i + 1) * 128],
                           xn_fm[:, i * T + tblk * 512: i * T + (tblk + 1) * 512],
                           start=(i == 0), stop=(i == NI - 1), waits=w)
                    tok_pq.append(p.pe_drain())
                    tq1 = p.add("vector", lambda e, b=bank, j=jj: e.tensor_scalar(
                        sc1, b[:, 0:512], bq_pp[:, j:j + 1], 0.0,
                        op0=ALU.add, op1=ALU.min), waits=[tok_pq[-1]])
                    tq2 = p.add("scalar", lambda e: e.activation(sc2, sc1, AF.Exp),
                                waits=[tq1])
                    p.add("vector", lambda e, b=bank, j=jj: e.tensor_scalar(
                        sc3, b[:, 0:512], bq_pp[:, j:j + 1], 0.0,
                        op0=ALU.add, op1=ALU.max))
                    tq4 = p.add(
                        "vector",
                        lambda e, o=qf_fm[:, hh * T + tblk * 512: hh * T + (tblk + 1) * 512]:
                            e.tensor_add(o, sc2, sc3),
                        waits=[tq2])
                    tok_qevac.append(tq4)
                    qi += 1
            prev_p1c_drain = tok_pq[-1]
            tok_qdone = tok_qevac[-1]

            if debug and g == 0:
                p.dma("sync", lambda e: e.dma_start(out=dbg_qf, in_=qf_fm),
                      waits=[tok_qdone])

            t_ms = p.add("vector", lambda e: e.memset(state[:, :], 0.0),
                         waits=[prev_norm_last, prev_cast_last])
            tok_gr = [None] * 8
            t_acc = t_ms
            for r in range(8):
                gs = gath[:, :]
                tok_gr[r] = p.dma(
                    "sync",
                    lambda e, o=gs, i=cc_out[g][r * 128:(r + 1) * 128, :]: e.dma_start(out=o, in_=i),
                    waits=[tok_cc[g], t_acc if r >= 1 else None])
                t_acc = p.add(
                    "vector",
                    lambda e, i=gs, r_=r: e.scalar_tensor_tensor(
                        out=state[:, :], in0=i, scalar=rm_sb[:, r_:r_ + 1], in1=state[:, :],
                        op0=ALU.mult, op1=ALU.add),
                    waits=[tok_gr[r]])
            tok_state = p.add("vector", lambda e: e.tensor_copy(state_bf[:, :], state[:, :]),
                              waits=[t_acc, prev_group_pe])
            if debug:
                p.dma("sync", lambda e, o=dbg_st[g]: e.dma_start(out=o, in_=state[:, :]),
                      waits=[tok_state])

            t_kf4 = None
            t_mask = None
            t_sadd_last = None
            t_cast = None
            t_norm = None
            t_stage = None
            o_sl = []
            for hh in range(HPG):
                bank = [s2, s3][hh // 2]
                co = (hh % 2) * 256
                o_sl.append(bank[:, co:co + 129])
            kv_sl = []
            for hh in range(HPG):
                bank = [s1, d1][hh // 2]
                co = (hh % 2) * 256
                kv_sl.append(bank[:, co:co + 129])

            for n in range(NT):
                first = (n == 0)
                for hh in range(HPG):
                    w = []
                    if hh == 0:
                        w = [t_kf4]
                        if first:
                            w += [prev_ccsb, tok_ek[NT - 1], tok_ev[NT - 1]]
                    TR(d0b[:, hh * 128:(hh + 1) * 128],
                       kf_tm[:, n * 512 + hh * 128: n * 512 + (hh + 1) * 128], waits=w)
                d1tok = p.pe_drain()
                t_kf4 = p.add("vector", lambda e: e.tensor_copy(kf_fm4[:, :], d0b[:, 0:512]),
                              waits=[d1tok])
                for hh in range(HPG):
                    w = [tok_state if first else t_cast]
                    if first and hh == 0:
                        w += [tok_qdone, t_norm, prev_norm_last]
                    else:
                        w += [t_norm]
                    qt = qf_fm[:, hh * T + n * 128: hh * T + (n + 1) * 128]
                    MM(o_sl[hh], qt, state_bf[:, hh * SW: hh * SW + 129],
                       start=(hh % 2 == 0), stop=False, waits=w)
                for hh in range(HPG):
                    kt = kf_tm[:, n * 512 + hh * 128: n * 512 + (hh + 1) * 128]
                    vt = vc_tm[:, n * 512 + hh * 128: n * 512 + (hh + 1) * 128]
                    w = [t_sadd_last] if hh == 0 else []
                    MM(kv_sl[hh][:, 0:128], kt, vt, start=(hh % 2 == 0), stop=False, waits=w)
                    MM(kv_sl[hh][:, 128:129], kt, ones_col, start=False,
                       stop=(hh % 2 == 1))
                for hh in range(HPG):
                    qt = qf_fm[:, hh * T + n * 128: hh * T + (n + 1) * 128]
                    w = [t_kf4, t_mask] if hh == 0 else []
                    MM(s0[:, hh * 128:(hh + 1) * 128], kf_fm4[:, hh * 128:(hh + 1) * 128],
                       qt, start=(hh == 0), stop=(hh == HPG - 1), waits=w)
                d2tok = p.pe_drain()
                t_mask = p.add("vector", lambda e: e.tensor_tensor(
                    A_m4[:, :], s0[:, 0:512], mask4[:, :], op=ALU.mult), waits=[d2tok])
                for hh in range(HPG):
                    t_sadd_last = p.add(
                        "vector",
                        lambda e, sl=state[:, hh * SW: hh * SW + 129], kv=kv_sl[hh]:
                            e.tensor_add(sl, sl, kv),
                        waits=[d2tok])
                t_cast = p.add("scalar", lambda e: e.copy(state_bf[:, :], state[:, :]),
                               waits=[t_sadd_last])
                for hh in range(HPG):
                    vt = vc_tm[:, n * 512 + hh * 128: n * 512 + (hh + 1) * 128]
                    w = [t_mask] if hh == 0 else []
                    MM(o_sl[hh][:, 0:128], A_m4[:, hh * 128:(hh + 1) * 128], vt,
                       start=False, stop=False, waits=w)
                    MM(o_sl[hh][:, 128:129], A_m4[:, hh * 128:(hh + 1) * 128], ones_col,
                       start=False, stop=(hh % 2 == 1))
                d3tok = p.pe_drain()
                for hh in range(HPG):
                    p.add("vector", lambda e, o=ztmp[:, hh:hh + 1], i=o_sl[hh][:, 128:129]:
                          e.tensor_scalar_add(o, i, 1e-6),
                          waits=[d3tok] if hh == 0 else [])
                p.add("vector", lambda e: e.reciprocal(rz[:, :], ztmp[:, :]))
                for hh in range(HPG):
                    t_norm = p.add(
                        "vector",
                        lambda e, o=attn_tm4[:, hh * 128:(hh + 1) * 128],
                        i=o_sl[hh][:, 0:128], r=rz[:, hh:hh + 1]: e.tensor_scalar_mul(o, i, r))
                for hh in range(HPG):
                    w = []
                    if hh == 0:
                        w = [t_norm, t_stage]
                    TR(d0b[:, 1024 + hh * 128: 1024 + (hh + 1) * 128],
                       attn_tm4[:, hh * 128:(hh + 1) * 128], waits=w)
                d4tok = p.pe_drain()
                t_stage = p.add(
                    "vector",
                    lambda e, o=astage[:, (n % 2) * 512:(n % 2) * 512 + 512]:
                        e.tensor_copy(o, d0b[:, 1024:1536]),
                    waits=[d4tok,
                           tok_attn_dma_n.get((g, n - 2)) if n >= 2
                           else tok_attn_dma_n.get((g - 1, NT - 2 + n))])
                out_ap = attn_sc[g * HPG:(g + 1) * HPG, :, n * 128:(n + 1) * 128] \
                    .transpose([1, 0, 2])
                in_ap = astage[:, (n % 2) * 512:(n % 2) * 512 + 512] \
                    .rearrange("p (h c) -> p h c", c=128)
                tok_attn_dma_n[(g, n)] = p.dma(
                    "sync",
                    lambda e, o=out_ap, i=in_ap: e.dma_start(out=o, in_=i),
                    waits=[t_stage])

            prev_group_pe = p.pe_drain()
            prev_norm_last = t_norm
            prev_cast_last = t_cast
            prev_stage_last = t_stage

        tok_attn_all = ("qs", p.counts["qs"])

        tok_p15 = {}
        tok_yev15 = {}
        tok_yw15 = {}
        prev_ares_reader = None
        tok_ares = None
        for tblk in range(4):
            for jc in range(16):
                tok_ares = p.dma(
                    "sync",
                    lambda e, o=attn_res[:, jc * 512:(jc + 1) * 512],
                    i=attn_sc[jc, :, tblk * 512:(tblk + 1) * 512]: e.dma_start(out=o, in_=i),
                    waits=[tok_attn_all if (tblk == 0 and jc == 0) else None,
                           prev_group_pe if (tblk == 0 and jc == 0) else None,
                           prev_ares_reader if jc == 0 else None])
            for dc in range(16):
                t_wo = p.dma(
                    "gpsimd",
                    lambda e, o=wo_sl[dc % 2], i=wo_d[dc]: e.dma_start(out=o, in_=i),
                    waits=[tok_p15.get((tblk, dc - 2)) or tok_p15.get((tblk - 1, dc + 14)),
                           tok_attn_all if (tblk == 0 and dc == 0) else None,
                           prev_group_pe if (tblk == 0 and dc == 0) else None])
                t_xr = p.dma(
                    "sync",
                    lambda e, o=xr15[dc % 2],
                    i=xt_d[dc * 128:(dc + 1) * 128, tblk * 512:(tblk + 1) * 512]:
                        e.dma_start(out=o, in_=i),
                    waits=[tok_yev15.get((tblk, dc - 2)) or tok_yev15.get((tblk - 1, dc + 14))])
                bank = [s0, s1][dc % 2]
                for jc in range(16):
                    w = []
                    if jc == 0:
                        w = [t_wo, tok_ares,
                             tok_yev15.get((tblk, dc - 2)) or tok_yev15.get((tblk - 1, dc + 14))]
                        if tblk == 0 and dc == 0:
                            w += [prev_group_pe, ("dv", p.counts["dv"])]
                    MM(bank[:, 0:512], wo_sl[dc % 2][:, jc * 128:(jc + 1) * 128],
                       attn_res[:, jc * 512:(jc + 1) * 512],
                       start=(jc == 0), stop=(jc == 15), waits=w)
                tok_p15[(tblk, dc)] = p.pe_drain()
                t_ye = p.add(
                    "vector",
                    lambda e, o=ysb15[dc % 2], b=bank, d=dc, xr=xr15[dc % 2]:
                        e.scalar_tensor_tensor(out=o, in0=b[:, 0:512],
                                               scalar=bo_pp[:, d:d + 1], in1=xr,
                                               op0=ALU.add, op1=ALU.add),
                    waits=[tok_p15[(tblk, dc)], t_xr,
                           tok_yw15.get((tblk, dc - 2)) or tok_yw15.get((tblk - 1, dc + 14))])
                tok_yev15[(tblk, dc)] = t_ye
                tok_yw15[(tblk, dc)] = p.dma(
                    "sync",
                    lambda e, o=y_d[dc * 128:(dc + 1) * 128, tblk * 512:(tblk + 1) * 512],
                    i=ysb15[dc % 2]: e.dma_start(out=o, in_=i),
                    waits=[t_ye])
            prev_ares_reader = tok_p15[(tblk, 15)]

        tok_dbg15 = None
        if debug:
            for dc in range(16):
                tw = p.dma("sync",
                           lambda e, o=ysb15[dc % 2],
                           i=y_d[dc * 128:(dc + 1) * 128, 0:512]: e.dma_start(out=o, in_=i),
                           waits=[tok_yw15[(3, 15)],
                                  tok_dbg15 if dc >= 2 else None])
                tok_dbg15 = p.dma("sync",
                                  lambda e, o=dbg_y15[dc * 128:(dc + 1) * 128, 0:512],
                                  i=ysb15[dc % 2]: e.dma_start(out=o, in_=i),
                                  waits=[tw])

        tok_yw2 = {}
        tok_sil = {}
        tok_gt = {}
        tok_pg = {}
        tok_pu = {}
        tok_pd = {}
        tok_ye2 = {}
        first_gate_guards = [tok_p15[(3, 15)], ("dv", p.counts["dv"])]

        for tau in range(2):
            for half in range(2):
                ph = (tau, half)
                for hcL in range(32):
                    hc = half * 32 + hcL
                    fallback_guard = (tok_pd.get((tau, half - 1, 15)) or
                                      tok_pd.get((tau - 1, 1, 15)) or
                                      tok_p15.get((3, 15)))
                    t_wg = p.dma(
                        "gpsimd",
                        lambda e, o=wg_sl[hcL % 2], i=wg_d[hc]: e.dma_start(out=o, in_=i),
                        waits=[tok_pg.get((ph, hcL - 2)) or fallback_guard])
                    t_wu = p.dma(
                        "gpsimd",
                        lambda e, o=wu_sl[hcL % 2], i=wu_d[hc]: e.dma_start(out=o, in_=i),
                        waits=[tok_pu.get((ph, hcL - 2)) or fallback_guard])
                    for i in range(NI):
                        w = []
                        if i == 0:
                            w = [t_wg, tok_sil.get((ph, hcL - 1))]
                            if hcL == 0:
                                w += first_gate_guards
                                w.append(tok_ye2.get((tau, half - 1, 15)))
                                w.append(tok_ye2.get((tau - 1, 1, 15)))
                                w.append(tok_dbg15)
                        for cb in range(2):
                            MM(d0[:, cb * 512:(cb + 1) * 512],
                               wg_sl[hcL % 2][:, i * 128:(i + 1) * 128],
                               xn_fm[:, i * T + tau * 1024 + cb * 512:
                                     i * T + tau * 1024 + (cb + 1) * 512],
                               start=(i == 0), stop=(i == NI - 1),
                               waits=w if cb == 0 else [])
                    tok_pg[(ph, hcL)] = p.pe_drain()
                    for i in range(NI):
                        w = []
                        if i == 0:
                            w = [t_wu, tok_gt.get((ph, hcL - 1))]
                            if hcL == 0:
                                w += first_gate_guards
                                w.append(tok_ye2.get((tau, half - 1, 15)))
                                w.append(tok_ye2.get((tau - 1, 1, 15)))
                        for cb in range(2):
                            MM(d1[:, cb * 512:(cb + 1) * 512],
                               wu_sl[hcL % 2][:, i * 128:(i + 1) * 128],
                               xn_fm[:, i * T + tau * 1024 + cb * 512:
                                     i * T + tau * 1024 + (cb + 1) * 512],
                               start=(i == 0), stop=(i == NI - 1),
                               waits=w if cb == 0 else [])
                    tok_pu[(ph, hcL)] = p.pe_drain()
                    tok_sil[(ph, hcL)] = p.add(
                        "scalar",
                        lambda e: e.activation(gsil[:, 0:1024], d0[:, 0:1024], AF.Silu),
                        waits=[tok_pg[(ph, hcL)], tok_gt.get((ph, hcL - 1)),
                               tok_yev15.get((3, 15)) if (ph == (0, 0) and hcL == 0)
                               else None])
                    tok_gt[(ph, hcL)] = p.add(
                        "vector",
                        lambda e, o=gated[:, hcL * 1024:(hcL + 1) * 1024]:
                            e.tensor_tensor(o, d1[:, 0:1024], gsil[:, 0:1024], op=ALU.mult),
                        waits=[tok_pu[(ph, hcL)], tok_sil[(ph, hcL)]])
                tok_gt_all = tok_gt[(ph, 31)]
                if debug and ph == (0, 0):
                    tok_gt_all = p.dma("sync", lambda e: e.dma_start(
                        out=dbg_gated, in_=gated), waits=[tok_gt[(ph, 31)]])
                for dc in range(16):
                    t_wd = p.dma(
                        "gpsimd",
                        lambda e, o=wd_sl[dc % 2],
                        i=wd_d[dc][:, half * 4096:(half + 1) * 4096]: e.dma_start(out=o, in_=i),
                        waits=[tok_pd.get((tau, half, dc - 2)) or tok_p15.get((3, 15))])
                    t_yin = p.dma(
                        "sync",
                        lambda e, o=yin2[dc % 2],
                        i=y_d[dc * 128:(dc + 1) * 128, tau * 1024:(tau + 1) * 1024]:
                            e.dma_start(out=o, in_=i),
                        waits=[tok_yw2.get((tau, half - 1, dc)) if half == 1 else
                               tok_yw15.get((2 * tau + 1, dc)),
                               tok_yw15.get((2 * tau, dc)) if half == 0 else None,
                               tok_yw15.get((3, 15)) if (tau, half) == (0, 0) else None,
                               tok_dbg15 if (tau, half) == (0, 0) else None,
                               tok_ye2.get((tau, half, dc - 2))])
                    bank = [d0, d1][dc % 2]
                    for hcL in range(32):
                        w = []
                        if hcL == 0:
                            w = [t_wd, tok_gt_all, tok_sil[(ph, 31)],
                                 tok_ye2.get((tau, half, dc - 2))]
                        for cb in range(2):
                            MM(bank[:, cb * 512:(cb + 1) * 512],
                               wd_sl[dc % 2][:, hcL * 128:(hcL + 1) * 128],
                               gated[:, hcL * 1024 + cb * 512: hcL * 1024 + (cb + 1) * 512],
                               start=(hcL == 0), stop=(hcL == 31),
                               waits=w if cb == 0 else [])
                    tok_pd[(tau, half, dc)] = p.pe_drain()
                    t_ye = p.add(
                        "vector",
                        lambda e, o=ysb2[dc % 2], b=bank, yi=yin2[dc % 2]:
                            e.tensor_add(o, b[:, 0:1024], yi),
                        waits=[tok_pd[(tau, half, dc)], t_yin,
                               tok_yw2.get((tau, half, dc - 2))])
                    tok_ye2[(tau, half, dc)] = t_ye
                    tok_yw2[(tau, half, dc)] = p.dma(
                        "sync",
                        lambda e, o=y_d[dc * 128:(dc + 1) * 128, tau * 1024:(tau + 1) * 1024],
                        i=ysb2[dc % 2]: e.dma_start(out=o, in_=i),
                        waits=[t_ye])
                first_gate_guards = [tok_pd[(tau, half, 15)]]

        final_qs = p.counts["qs"]
        final_qg = p.counts["qg"]
        p.add("sync", lambda e: e.nop(), waits=[("qs", final_qs)])
        p.add("gpsimd", lambda e: e.nop(), waits=[("qg", final_qg)], is_dma=True)

        def emit(eng_obj, name):
            for fn, waits, rec in p.items[name]:
                for s, v in waits:
                    eng_obj.wait_ge(sems[s], v)
                inst = fn(eng_obj)
                if rec is not None:
                    inst.then_inc(sems[rec[0]], rec[1])

        @block.sync
        def _(e):
            emit(e, "sync")

        @block.gpsimd
        def _(e):
            emit(e, "gpsimd")

        @block.vector
        def _(e):
            emit(e, "vector")

        @block.scalar
        def _(e):
            emit(e, "scalar")

        @block.tensor
        def _(e):
            emit(e, "tensor")

    return nc


_NC_CACHE = {}


def _get_nc(debug=DEBUG):
    if debug not in _NC_CACHE:
        _NC_CACHE[debug] = build(debug)
    return _NC_CACHE[debug]


def make_in_maps(L0_attn_x, L0_attn_Wq, L0_attn_bq, L0_attn_Wk, L0_attn_bk,
                 L0_attn_Wv, L0_attn_bv, L0_attn_Wo, L0_attn_bo,
                 L0_ffn_w_gate, L0_ffn_w_up, L0_ffn_w_down):
    f = np.float32
    x = np.asarray(L0_attn_x, f)
    Wq = np.asarray(L0_attn_Wq, f); Wk = np.asarray(L0_attn_Wk, f)
    Wv = np.asarray(L0_attn_Wv, f); Wo = np.asarray(L0_attn_Wo, f)
    bq = np.asarray(L0_attn_bq, f); bk = np.asarray(L0_attn_bk, f)
    bv = np.asarray(L0_attn_bv, f); bo = np.asarray(L0_attn_bo, f)
    wg = np.asarray(L0_ffn_w_gate, f); wu = np.asarray(L0_ffn_w_up, f)
    wd = np.asarray(L0_ffn_w_down, f)

    wq_p = np.ascontiguousarray(
        Wq.reshape(16, 128, 16, 128).transpose(2, 1, 0, 3).reshape(16, 128, 2048))
    wk_p = np.ascontiguousarray(
        Wk.reshape(16, 128, 4, 512).transpose(2, 1, 0, 3).reshape(4, 128, 8192))
    wv_p = np.ascontiguousarray(
        Wv.reshape(16, 128, 4, 512).transpose(2, 1, 0, 3).reshape(4, 128, 8192))
    wo_p = np.ascontiguousarray(
        Wo.reshape(16, 128, 16, 128).transpose(2, 1, 0, 3).reshape(16, 128, 2048))
    wg_p = np.ascontiguousarray(
        wg.reshape(64, 128, 16, 128).transpose(0, 3, 2, 1).reshape(64, 128, 2048))
    wu_p = np.ascontiguousarray(
        wu.reshape(64, 128, 16, 128).transpose(0, 3, 2, 1).reshape(64, 128, 2048))
    wd_p = np.ascontiguousarray(
        wd.reshape(16, 128, 64, 128).transpose(0, 3, 2, 1).reshape(16, 128, 8192))
    bq_pp = np.ascontiguousarray(bq.reshape(16, 128).T)
    bo_pp = np.ascontiguousarray(bo.reshape(16, 128).T)
    bk_r = bk.reshape(1, 2048)
    bv_r = bv.reshape(1, 2048)

    in_maps = []
    for core in range(8):
        b = core // 4
        t0 = (core % 4) * T
        xs = np.ascontiguousarray(x[b, t0:t0 + T, :])
        xts = np.ascontiguousarray(xs.T)
        rmask = np.zeros((128, 8), f)
        for j in range(4 * (core // 4), core):
            rmask[:, j] = 1.0
        in_maps.append({
            "x": xs, "xt": xts, "wq": wq_p, "wk": wk_p, "wv": wv_p,
            "wo": wo_p, "wg": wg_p, "wu": wu_p, "wd": wd_p,
            "bqp": bq_pp, "bop": bo_pp, "bk": bk_r, "bv": bv_r,
            "rmask": rmask,
        })
    return in_maps


def kernel(**inputs):
    nc = _get_nc(False)
    in_maps = make_in_maps(**inputs)
    res = run_bass_kernel_spmd(nc, in_maps, core_ids=list(range(8)))
    out = np.empty((2, 8192, D), np.float32)
    for core in range(8):
        b = core // 4
        t0 = (core % 4) * T
        out[b, t0:t0 + T, :] = np.asarray(res.results[core]["y"]).T
    return out
